# version 03
# baseline (speedup 1.0000x reference)
import numpy as np
import ml_dtypes
from contextlib import ExitStack

import concourse.bass as bass
import concourse.mybir as mybir
from concourse.bass_utils import run_bass_kernel_spmd
from concourse.alu_op_type import AluOpType as OP

F32 = mybir.dt.float32
BF16 = mybir.dt.bfloat16
I32 = mybir.dt.int32
F8 = mybir.dt.float8e4
AF = mybir.ActivationFunctionType
AXX = mybir.AxisListType.X

S = 4096
DM = 1024
DH = 4096
NT = S // 128
MASKVAL = -3.2e10

ENGS = ("sync", "tensor", "vector", "scalar", "gpsimd")


class Sem:
    def __init__(self, nc, es, name):
        self.h = es.enter_context(nc.semaphore(name))
        self.n = 0


def _mk_body(lst):
    def body(eng):
        for f in lst:
            f(eng)
    return body


class Phase:

    def __init__(self, bld, name):
        self.bld = bld
        self.name = name
        self.q = {e: [] for e in ENGS}
        self.snap = {k: v.n for k, v in bld._sems.items()} if hasattr(bld, "_sems") else {}

    def op(self, eng, mk, inc=None, k=1, waits=()):
        lst = self.q[eng]
        for (s, v) in waits:
            if v > 0:
                lst.append(lambda e, _s=s.h, _v=v: e.wait_ge(_s, _v))
        if inc is not None:
            inc.n += k
            lst.append(lambda e, _mk=mk, _s=inc.h, _k=k: _mk(e).then_inc(_s, _k))
            return inc.n
        lst.append(lambda e, _mk=mk: _mk(e))
        return None

    def wait(self, eng, sem, v=None):
        if v is None:
            sem, v = sem
        if v > 0:
            self.q[eng].append(lambda e, _s=sem.h, _v=v: e.wait_ge(_s, _v))

    def realize(self):
        for nm, sem in self.bld._sems.items():
            if nm.startswith("dma_") and sem.n > self.snap.get(nm, 0):
                q = self.bld._stream_queue.get(nm, "gpsimd")
                self.wait(q, sem, sem.n)
        with self.bld.nc.Block(self.name) as blk:
            for e in ENGS:
                if self.q[e]:
                    getattr(blk, e)(_mk_body(self.q[e]))


class Build:
    def __init__(self, phases=("p0", "p1a", "p1b", "p2", "p3a", "p3b", "p4"),
                 dbg_in=(), dbg_out=()):
        self.phases = phases
        self.dbg_in = set(dbg_in)
        self.dbg_out = set(dbg_out)
        nc = self.nc = bass.Bass("TRN2", target_bir_lowering=False)
        self.es = ExitStack()

        d = self.d = {}

        def dram(name, shape, dt, kind):
            d[name] = nc.dram_tensor(name, shape, dt, kind=kind).ap()

        dram("x", [S, DM], F32, "ExternalInput")
        for n in ("wq", "wk", "wv", "wo"):
            dram("a_" + n, [DM, DM], F32, "ExternalInput")
        for n in ("bq", "bk", "bv", "bo"):
            dram("a_" + n, [DM], F32, "ExternalInput")
        for n in ("Wq", "Wk", "Wv", "Wo"):
            dram("b_" + n, [DM, DM], F32, "ExternalInput")
        dram("b_bo", [DM], F32, "ExternalInput")
        dram("f_w1", [DH, DM], F32, "ExternalInput")
        dram("f_cw", [DH, 1, 3], F32, "ExternalInput")
        dram("f_cb", [DH], F32, "ExternalInput")
        dram("f_w2", [DM, DH], F32, "ExternalInput")
        dram("c_identbf", [128, 128], BF16, "ExternalInput")
        dram("c_identf", [128, 128], F32, "ExternalInput")
        dram("c_mm", [128, 128], F32, "ExternalInput")
        dram("c_mb", [128, 128], F32, "ExternalInput")
        dram("c_mm1", [128, 128], F32, "ExternalInput")
        dram("c_mb1", [128, 128], F32, "ExternalInput")
        dram("c_bqk", [128, 128], F32, "ExternalInput")
        dram("c_convw", [128, 128], F32, "ExternalInput")
        dram("c_convb", [128, 128], F32, "ExternalInput")
        dram("c_boa", [128, DM], F32, "ExternalInput")
        dram("c_bob", [128, DM], F32, "ExternalInput")
        dram("c_bva", [128, DM], F32, "ExternalInput")
        dram("y", [S, DM], F32, "ExternalOutput")

        def scratch(name, shape, dt):
            kind = "Internal"
            if name in self.dbg_in:
                kind = "ExternalInput"
            elif name in self.dbg_out:
                kind = "ExternalOutput"
            d[name] = nc.dram_tensor(name, shape, dt, kind=kind).ap()

        scratch("x1s", [S, DM], F32)
        scratch("x2s", [S, DM], F32)
        scratch("x3s", [S, DM], F32)
        scratch("qaT", [DM, S], BF16)
        scratch("kaT", [DM, S], BF16)
        scratch("qbT", [DM, S], BF16)
        scratch("kbT", [DM, S], BF16)
        scratch("vbp", [S, 2080], BF16)
        scratch("wqTb", [DM, DM], BF16)
        scratch("wkTb", [DM, DM], BF16)
        scratch("wvTb", [DM, DM], BF16)
        scratch("woTb", [DM, DM], BF16)
        scratch("w1t8", [DM, DH], F8)
        scratch("w2t8", [DH, DM], F8)
        scratch("meanw", [1, 2], F32)
        scratch("srow", [1, S + 2], F32)
        scratch("s2row", [1, S], F32)
        scratch("xqTs", [DM, S], BF16)
        scratch("denrow", [16, S], F32)
        scratch("pvrow", [1, 128], BF16)

        es = self.es
        self.sP = Sem(nc, es, "sP")
        self.sD = Sem(nc, es, "sD")
        self.sA = Sem(nc, es, "sA")
        self._sems = {}

        g = self.g = {}

        def sb(name, shape, dt):
            g[name] = es.enter_context(nc.sbuf_tensor(name, shape, dt))

        sb("identbf", [128, 128], BF16)
        sb("identf", [128, 128], F32)
        sb("bqk", [128, 128], F32)
        sb("convw", [128, 128], F32)
        sb("convb", [128, 128], F32)
        sb("mw1", [128, 1], F32)
        sb("mw2", [128, 1], F32)
        sb("onesf", [128, 1], F32)
        sb("ones1f", [1, 128], F32)
        sb("ones1b", [1, 128], BF16)

    def sem(self, name):
        if name not in self._sems:
            self._sems[name] = Sem(self.nc, self.es, name)
        return self._sems[name]

    NOW = None

    def dma(self, ph, queue, stream, out, in_, waits=(), slow=False):
        qs = self.sem("dma_" + stream)
        if not hasattr(self, "_stream_queue"):
            self._stream_queue = {}
        self._stream_queue["dma_" + stream] = queue
        v = ph.op(queue, lambda e, o=out, i=in_, sl=slow: e.dma_start(
            out=o, in_=i, allow_slow_non_contiguous=sl), inc=qs, k=16, waits=waits)
        return (qs, v)

    def dr(self, ph):
        ph.op("vector", lambda e: e.drain())

    def nr_recip(self, ph, r, m, tmp):
        self.dr(ph)
        ph.op("vector", lambda e: e.tensor_tensor(tmp, m, r, OP.mult))
        self.dr(ph)
        ph.op("vector", lambda e: e.tensor_scalar(tmp, tmp, -1.0, 2.0, OP.mult, OP.add))
        self.dr(ph)
        ph.op("vector", lambda e: e.tensor_tensor(r, r, tmp, OP.mult))
        self.dr(ph)

    def nr_rsqrt(self, ph, r, t, tmp):
        self.dr(ph)
        ph.op("vector", lambda e: e.tensor_tensor(tmp, r, r, OP.mult))
        self.dr(ph)
        ph.op("vector", lambda e: e.tensor_tensor(tmp, tmp, t, OP.mult))
        self.dr(ph)
        ph.op("vector", lambda e: e.tensor_scalar(tmp, tmp, -0.5, 1.5, OP.mult, OP.add))
        self.dr(ph)
        ph.op("vector", lambda e: e.tensor_tensor(r, r, tmp, OP.mult))
        self.dr(ph)

    def ln_stats_group(self, ph, tiles, scr, S4, rr4, nb4, ld_waits, free_wait,
                       quant=None):
        sD, sA = self.sD, self.sA
        g = self.g
        sq = 0
        for ti, (tile, ldw) in enumerate(zip(tiles, ld_waits)):
            w = list(ldw)
            if ti == 0:
                w += list(free_wait)
            ph.op("vector", lambda e, t=tile, ti=ti: e.tensor_reduce(
                S4[:, ti:ti + 1], t, AXX, OP.add), waits=w)
            if quant is not None:
                ph.op("vector", lambda e, t=tile, ti=ti: e.tensor_reduce(
                    S4[:, 20 + ti:21 + ti], t, AXX, OP.max))
                ph.op("vector", lambda e, t=tile, ti=ti: e.tensor_reduce(
                    S4[:, 24 + ti:25 + ti], t, AXX, OP.min))
            sq = ph.op("scalar", lambda e, t=tile, ti=ti: e.activation(
                scr, t, AF.Square, accum_out=S4[:, 4 + ti:5 + ti]),
                inc=sA, waits=w)
        ph.wait("vector", sA, sq)
        self.dr(ph)
        ph.op("vector", lambda e: e.tensor_scalar(S4[:, 8:12], S4[:, 0:4], 1.0 / DM, None, OP.mult))
        ph.op("vector", lambda e: e.tensor_scalar(S4[:, 12:16], S4[:, 4:8], 1.0 / DM, None, OP.mult))
        self.dr(ph)
        ph.op("vector", lambda e: e.tensor_tensor(S4[:, 16:20], S4[:, 8:12], S4[:, 8:12], OP.mult))
        self.dr(ph)
        ph.op("vector", lambda e: e.tensor_tensor(S4[:, 12:16], S4[:, 12:16], S4[:, 16:20], OP.subtract))
        self.dr(ph)
        dv = ph.op("vector", lambda e: e.tensor_scalar(S4[:, 12:16], S4[:, 12:16], 1e-5, None, OP.add),
                   inc=sD)
        ph.op("scalar", lambda e: e.activation(nb4[:], S4[:, 12:16], AF.Ln), waits=[(sD, dv)])
        st = ph.op("scalar", lambda e: e.activation(rr4[:], nb4[:], AF.Exp, scale=-0.5), inc=sA)
        ph.wait("vector", sA, st)
        self.nr_rsqrt(ph, rr4[:], S4[:, 12:16], S4[:, 16:20])
        ph.op("vector", lambda e: e.tensor_tensor(nb4[:], S4[:, 8:12], rr4[:], OP.mult))
        self.dr(ph)
        dv = ph.op("vector", lambda e: e.tensor_scalar(nb4[:], nb4[:], -1.0, None, OP.mult), inc=sD)
        if quant is None:
            self.dr(ph)
            return dv
        rs4, nbs4, ivc4 = quant
        self.dr(ph)
        ph.op("vector", lambda e: e.tensor_tensor(S4[:, 28:32], S4[:, 20:24], S4[:, 8:12], OP.subtract))
        ph.op("vector", lambda e: e.tensor_tensor(S4[:, 32:36], S4[:, 8:12], S4[:, 24:28], OP.subtract))
        self.dr(ph)
        ph.op("vector", lambda e: e.tensor_tensor(S4[:, 28:32], S4[:, 28:32], S4[:, 32:36], OP.max))
        self.dr(ph)
        ph.op("vector", lambda e: e.tensor_tensor(S4[:, 28:32], S4[:, 28:32], rr4[:], OP.mult))
        self.dr(ph)
        dv = ph.op("vector", lambda e: e.tensor_scalar(S4[:, 28:32], S4[:, 28:32], 1e-5, None, OP.max),
                   inc=sD)
        ph.op("scalar", lambda e: e.activation(S4[:, 32:36], S4[:, 28:32], AF.Ln), waits=[(sD, dv)])
        st = ph.op("scalar", lambda e: e.activation(S4[:, 16:20], S4[:, 32:36], AF.Exp, scale=-1.0),
                   inc=sA)
        ph.wait("vector", sA, st)
        self.nr_recip(ph, S4[:, 16:20], S4[:, 28:32], S4[:, 32:36])
        ph.op("vector", lambda e: e.tensor_scalar(ivc4[:], S4[:, 28:32], 1.0 / 127.0, None, OP.mult))
        self.dr(ph)
        ph.op("vector", lambda e: e.tensor_scalar(ivc4[:], ivc4[:], g["mw1"][:], None, OP.mult))
        ph.op("vector", lambda e: e.tensor_scalar(S4[:, 16:20], S4[:, 16:20], 127.0, None, OP.mult))
        self.dr(ph)
        ph.op("vector", lambda e: e.tensor_tensor(rs4[:], rr4[:], S4[:, 16:20], OP.mult))
        ph.op("vector", lambda e: e.tensor_tensor(nbs4[:], nb4[:], S4[:, 16:20], OP.mult))
        self.dr(ph)
        dv = ph.op("vector", lambda e: e.engine_nop(), inc=sD)
        return dv

    def p0(self):
        nc, d, g = self.nc, self.d, self.g
        ph = Phase(self, "p0")
        sP, sD, sA = self.sP, self.sD, self.sA

        self.dma(ph, "sync", "c", g["identbf"][:], d["c_identbf"])
        self.dma(ph, "sync", "c", g["identf"][:], d["c_identf"])
        self.dma(ph, "sync", "c", g["bqk"][:], d["c_bqk"])
        self.dma(ph, "sync", "c", g["convw"][:], d["c_convw"])
        cload = self.dma(ph, "sync", "c", g["convb"][:], d["c_convb"])
        ph.wait("vector", cload)
        ph.op("vector", lambda e: e.memset(g["onesf"][:], 1.0))
        ph.op("vector", lambda e: e.memset(g["ones1f"][:], 1.0))
        ph.op("vector", lambda e: e.memset(g["ones1b"][:], 1.0))

        with ExitStack() as es:
            A = es.enter_context
            w0 = A(nc.sbuf_tensor("p0_w0", [128, 1024], F32))
            w1 = A(nc.sbuf_tensor("p0_w1", [128, 1024], F32))
            bf = A(nc.sbuf_tensor("p0_bf", [128, 512], BF16))
            i32 = A(nc.sbuf_tensor("p0_i32", [128, 512], I32))
            q8 = A(nc.sbuf_tensor("p0_q8", [128, 512], F8))
            acc = A(nc.sbuf_tensor("p0_acc", [128, 2], F32))
            accrow = A(nc.sbuf_tensor("p0_accrow", [2, 128], F32))
            meanst = A(nc.sbuf_tensor("p0_meanst", [2, 1], F32))
            mrow = A(nc.sbuf_tensor("p0_mrow", [1, 2], F32))
            acsum = A(nc.sbuf_tensor("p0_acsum", [2, 1], F32))
            reds = A(nc.sbuf_tensor("p0_reds", [128, 64], F32))
            red = A(nc.sbuf_tensor("p0_red", [128, 1], F32))
            sca = A(nc.sbuf_tensor("p0_sca", [128, 1], F32))
            scb = A(nc.sbuf_tensor("p0_scb", [128, 1], F32))
            pst = [A(nc.psum_tensor(f"p0_pst{i}", [128, 512], F32)) for i in range(2)]
            pss = A(nc.psum_tensor("p0_pss", [128, 512], F32))

            bufs = (w0, w1)
            last_use = {}

            last_read = [0, 0]
            nbuf = 0
            for src, dst in (("b_Wq", "wqTb"), ("b_Wk", "wkTb"),
                             ("b_Wv", "wvTb"), ("b_Wo", "woTb")):
                for i in range(8):
                    buf = bufs[nbuf % 2]
                    ld = self.dma(ph, "sync", f"p0w{nbuf % 2}", buf[:],
                                  d[src][i * 128:(i + 1) * 128, :],
                                  waits=[(sP, last_use.get(nbuf % 2, 0))])
                    for grp in range(2):
                        bank = pst[grp]
                        tp = 0
                        for j in range(grp * 4, grp * 4 + 4):
                            w = [ld] if j == grp * 4 else ()
                            if j == grp * 4:
                                w = list(w) + [(sD, last_read[grp])]
                            tp = ph.op("tensor", lambda e, b=buf, j=j, bk=bank:
                                       e.transpose(bk[:, (j % 4) * 128:(j % 4 + 1) * 128],
                                                   b[:, j * 128:(j + 1) * 128], g["identf"][:]),
                                       inc=sP, waits=w)
                        cv = ph.op("vector", lambda e, bk=bank: e.tensor_copy(bf[:], bk[:]),
                                   inc=sD, waits=[(sP, tp),
                                                  getattr(self, "_p0_bfst", (sD, 0))])
                        last_read[grp] = cv
                        stv = 0
                        for j in range(grp * 4, grp * 4 + 4):
                            stv = self.dma(ph, "gpsimd", "p0bf",
                                           d[dst][j * 128:(j + 1) * 128, i * 128:(i + 1) * 128],
                                           bf[:, (j % 4) * 128:(j % 4 + 1) * 128],
                                           waits=[(sD, cv)] if j == grp * 4 else ())
                        self._p0_bfst = stv
                    last_use[nbuf % 2] = tp
                    nbuf += 1

            for wi, src in enumerate(("f_w1", "f_w2")):
                flat = d[src].rearrange("r c -> (r c)")
                for i in range(32):
                    buf = bufs[i % 2]
                    w_ = [(sD, last_use.get(("m", wi * 32 + i - 2), 0))]
                    if wi == 0 and i < 2:
                        w_.append((sP, last_use.get(i % 2, 0)))
                    ld = self.dma(ph, "sync", f"p0w{i % 2}", buf[:],
                                  flat[i * 131072:(i + 1) * 131072].rearrange("(p c) -> p c", p=128),
                                  waits=w_)
                    rd = ph.op("vector", lambda e, b=buf, col=wi * 32 + i: e.tensor_reduce(
                        reds[:, col:col + 1], b[:], AXX, OP.add, apply_absolute_value=True),
                        inc=sD, waits=[ld])
                    last_use[("m", wi * 32 + i)] = rd
            self.dr(ph)
            rd2 = 0
            for wi in range(2):
                rd2 = ph.op("vector", lambda e, wi=wi: e.tensor_reduce(
                    acc[:, wi:wi + 1], reds[:, wi * 32:(wi + 1) * 32], AXX, OP.add), inc=sD)
            self.dr(ph)
            mm = ph.op("tensor", lambda e: e.transpose(pss[0:2, 0:128], acc[:], g["identf"][:]),
                       inc=sP, waits=[(sD, rd2)])
            mv = ph.op("vector", lambda e: e.tensor_copy(accrow[:], pss[0:2, 0:128]),
                       inc=sD, waits=[(sP, mm)])
            ac_ = ph.op("scalar", lambda e: e.activation(
                accrow[:], accrow[:], AF.Identity, accum_out=acsum[:]),
                inc=sA, waits=[(sD, mv)])
            mv = ph.op("vector", lambda e, n=(DH * DM): e.tensor_scalar(
                meanst[:], acsum[:], 1.0 / n, None, OP.mult), inc=sD, waits=[(sA, ac_)])
            self.dr(ph)
            mm = ph.op("tensor", lambda e: e.transpose(pss[0:1, 128:130], meanst[:], g["identf"][0:2, 0:2]),
                       inc=sP, waits=[(sD, mv)])
            mv = ph.op("vector", lambda e: e.tensor_copy(mrow[:], pss[0:1, 128:130]),
                       inc=sD, waits=[(sP, mm)])
            mm = ph.op("tensor", lambda e: e.matmul(pss[:, 130:132], g["ones1f"][:], mrow[:],
                                                    start=True, stop=True),
                       inc=sP, waits=[(sD, mv)])
            mv = ph.op("vector", lambda e: e.tensor_copy(g["mw1"][:], pss[:, 130:131]),
                       inc=sD, waits=[(sP, mm)])
            mv = ph.op("vector", lambda e: e.tensor_copy(g["mw2"][:], pss[:, 131:132]), inc=sD)
            self.dr(ph)
            self.dr(ph)
            dv_ = ph.op("vector", lambda e: e.tensor_scalar(sca[:], g["mw1"][:], 1e-5, None, OP.max),
                        inc=sD, waits=[(sD, mv)])
            dv_ = ph.op("vector", lambda e: e.tensor_scalar(scb[:], g["mw2"][:], 1e-5, None, OP.max),
                        inc=sD)
            self.dr(ph)
            ph.op("vector", lambda e: e.tensor_copy(acc[:, 0:1], sca[:]))
            dv_ = ph.op("vector", lambda e: e.tensor_copy(acc[:, 1:2], scb[:]), inc=sD)
            self.dr(ph)
            ph.op("scalar", lambda e: e.activation(red[:], sca[:], AF.Ln), waits=[(sD, dv_)])
            ph.op("scalar", lambda e: e.activation(sca[:], red[:], AF.Exp, scale=-1.0))
            ph.op("scalar", lambda e: e.activation(red[:], scb[:], AF.Ln))
            qs2 = ph.op("scalar", lambda e: e.activation(scb[:], red[:], AF.Exp, scale=-1.0), inc=sA)
            ph.wait("vector", sA, qs2)
            self.nr_recip(ph, sca[:], acc[:, 0:1], red[:])
            self.nr_recip(ph, scb[:], acc[:, 1:2], red[:])
            qs = ph.op("vector", lambda e: e.engine_nop(), inc=sD)

            last_q = 0
            q8st = (sD, 0)
            for wi, (src, dst, rows, cols) in enumerate((("f_w1", "w1t8", DH, DM),
                                                         ("f_w2", "w2t8", DM, DH))):
                for i in range(rows // 128):
                    for cp in range(cols // 1024):
                        buf = bufs[(i * (cols // 1024) + cp) % 2]
                        ld = self.dma(ph, "sync", f"p0w{(i * (cols // 1024) + cp) % 2}", buf[:],
                                      d[src][i * 128:(i + 1) * 128, cp * 1024:(cp + 1) * 1024],
                                      waits=[(sP, last_use.get(("q", (i * (cols // 1024) + cp) % 2), 0)),
                                             (sD, qs)])
                        for grp in range(2):
                            bank = pst[grp]
                            for j in range(grp * 4, grp * 4 + 4):
                                w = []
                                if j == grp * 4:
                                    w = [ld, (sD, getattr(self, "_p0_qa", {}).get(grp, 0))]
                                tp = ph.op("tensor", lambda e, b=buf, j=j, bk=bank:
                                           e.transpose(bk[:, (j % 4) * 128:(j % 4 + 1) * 128],
                                                       b[:, j * 128:(j + 1) * 128], g["identf"][:]),
                                           inc=sP, waits=w)
                            av = ph.op("vector", lambda e, bk=bank, s=(sca if wi == 0 else scb)[:]:
                                       e.tensor_scalar(i32[:], bk[:], s, None, OP.mult),
                                       inc=sD, waits=[(sP, tp), (sD, last_q)])
                            if not hasattr(self, "_p0_qa"):
                                self._p0_qa = {}
                            self._p0_qa[grp] = av
                            cv = ph.op("vector", lambda e: e.tensor_scalar(
                                q8[:], i32[:], -1.0, 1.0, OP.max, OP.min),
                                inc=sD, waits=[(sD, av), q8st])
                            last_q = cv
                            for j in range(grp * 4, grp * 4 + 4):
                                cglob = cp * 8 + j
                                q8st = self.dma(ph, "gpsimd", "p0q8",
                                                d[dst][cglob * 128:(cglob + 1) * 128,
                                                       i * 128:(i + 1) * 128],
                                                q8[:, (j % 4) * 128:(j % 4 + 1) * 128],
                                                waits=[(sD, cv)] if j == grp * 4 else ())
                        last_use[("q", (i * (cols // 1024) + cp) % 2)] = tp

            for nm in ("dma_p0bf", "dma_p0q8", "dma_p0m"):
                if nm in self._sems:
                    ph.wait("gpsimd", self._sems[nm], self._sems[nm].n)
            ph.realize()

    def p1a(self):
        nc, d, g = self.nc, self.d, self.g
        ph = Phase(self, "p1a")
        sP, sD, sA = self.sP, self.sD, self.sA
        A12 = self.es12.enter_context
        self.vA = A12(nc.sbuf_tensor("vA", [128, NT * DM], BF16))
        vA = self.vA

        with ExitStack() as es:
            A = es.enter_context
            wqa = A(nc.sbuf_tensor("wqa", [128, 8 * DM], BF16))
            wka = A(nc.sbuf_tensor("wka", [128, 8 * DM], BF16))
            wva = A(nc.sbuf_tensor("wva", [128, 8 * DM], BF16))
            bva_rep = A(nc.sbuf_tensor("p1a_bva", [128, DM], F32))
            sx = [A(nc.sbuf_tensor(f"p1a_x{i}", [128, 4096], F32)) for i in range(2)]
            scr = A(nc.sbuf_tensor("p1a_scr", [128, 1024], F32))
            sln = [A(nc.sbuf_tensor(f"p1a_ln{i}", [128, 1024], BF16)) for i in range(2)]
            lnT = [A(nc.sbuf_tensor(f"p1a_lnT{i}", [128, 4096], BF16)) for i in range(2)]
            S4 = A(nc.sbuf_tensor("p1a_S4", [128, 20], F32))
            rr4 = A(nc.sbuf_tensor("p1a_rr4", [128, 4], F32))
            nb4 = A(nc.sbuf_tensor("p1a_nb4", [128, 4], F32))
            stq = [A(nc.sbuf_tensor(f"p1a_stq{i}", [128, 512], BF16)) for i in range(2)]
            pst = [A(nc.psum_tensor(f"p1a_pst{i}", [128, 1024], BF16)) for i in range(2)]
            psq = [A(nc.psum_tensor(f"p1a_psq{i}", [128, 512], F32)) for i in range(2)]
            psv = A(nc.psum_tensor("p1a_psv", [128, 1024], F32))

            wld = (sD, 0)
            self.dma(ph, "gpsimd", "p1aw", bva_rep[:], d["c_bva"])
            for src, dst in (("a_wq", wqa), ("a_wk", wka), ("a_wv", wva)):
                wld = self.dma(ph, "gpsimd", "p1aw", dst[:].rearrange("p (dc o) -> p dc o", dc=8),
                               d[src].rearrange("(dc p) o -> p dc o", p=128))

            lnT_free = [0, 0]
            pst_free = [0, 0]
            sln_free = [0, 0]
            stq_free = [(sD, 0), (sD, 0)]
            stats_free = [0]
            x_free = [0, 0]
            bankg = [0, 0]
            for gi in range(8):
                xb = sx[gi % 2]
                ld = self.dma(ph, "sync", f"p1ax{gi % 2}", xb[:].rearrange("p (t c) -> p t c", t=4),
                              d["x"][gi * 512:(gi + 1) * 512, :].rearrange("(t p) c -> p t c", p=128),
                              waits=[(sA, x_free[gi % 2])])
                cv = 0
                sv = self.ln_stats_group(
                    ph, [xb[:, ti * 1024:(ti + 1) * 1024] for ti in range(4)],
                    scr[:], S4[:], rr4[:], nb4[:], [[ld]] * 4,
                    free_wait=[(sA, stats_free[0])])
                for ti in range(4):
                    b = ti % 2
                    tile = xb[:, ti * 1024:(ti + 1) * 1024]
                    lnb = sln[b]
                    av = ph.op("scalar", lambda e, t=tile, o=lnb, ti=ti: e.activation(
                        o[:], t, AF.Identity, bias=nb4[:, ti:ti + 1], scale=rr4[:, ti:ti + 1]),
                        inc=sA, waits=[(sD, sv), (sP, sln_free[b])])
                    stats_free[0] = av
                    if ti == 3:
                        x_free[gi % 2] = av
                    bank = pst[b]
                    tp = 0
                    for j in range(8):
                        w = [(sA, av), (sD, pst_free[b])] if j == 0 else ()
                        tp = ph.op("tensor", lambda e, j=j, lb=lnb, bk=bank: e.transpose(
                            bk[:, j * 128:(j + 1) * 128], lb[:, j * 128:(j + 1) * 128],
                            g["identbf"][:]), inc=sP, waits=w)
                    sln_free[b] = tp
                    w2 = [(sP, tp)]
                    if ti == 0:
                        w2.append((sP, lnT_free[gi % 2]))
                    cv = ph.op("vector", lambda e, bk=bank, lt=lnT[gi % 2], ti=ti: e.tensor_copy(
                        lt.rearrange("p (j c) -> p j c", j=8)[:, :, ti * 128:(ti + 1) * 128],
                        bk.rearrange("p (j c) -> p j c", j=8)),
                        inc=sD, waits=w2)
                    pst_free[b] = cv
                lnT_ready = cv
                lt = lnT[gi % 2]
                last = 0
                for proj, wsb in (("q", wqa), ("k", wka)):
                    for ofc in range(8):
                        b = ofc % 2
                        bank = psq[b]
                        for dc in range(8):
                            w = []
                            if dc == 0:
                                w = [(sD, lnT_ready), (sD, bankg[b])]
                                if gi == 0 and proj == "q" and ofc == 0:
                                    w.append(wld)
                            last = ph.op("tensor", lambda e, dc=dc, ofc=ofc, wsb=wsb, bk=bank, lt=lt:
                                         e.matmul(bk[:],
                                                  wsb[:, dc * 1024 + ofc * 128:dc * 1024 + (ofc + 1) * 128],
                                                  lt[:, dc * 512:(dc + 1) * 512],
                                                  start=(dc == 0), stop=(dc == 7)),
                                         inc=sP, waits=w)
                        if proj == "q":
                            stb = stq[b]
                            cv2 = ph.op("vector", lambda e, bk=bank, sb_=stb, ofc=ofc: e.tensor_scalar(
                                sb_[:], bk[:], g["bqk"][:, ofc:ofc + 1], None, OP.add),
                                inc=sD, waits=[(sP, last), stq_free[b]])
                            stq_free[b] = self.dma(
                                ph, "gpsimd", f"p1as{b}",
                                d["qaT"][ofc * 128:(ofc + 1) * 128, gi * 512:(gi + 1) * 512],
                                stb[:], waits=[(sD, cv2)])
                        else:
                            stb = stq[b]
                            cv2 = ph.op("vector", lambda e, bk=bank, sb_=stb, ofc=ofc: e.tensor_scalar(
                                sb_[:], bk[:], g["bqk"][:, 8 + ofc:9 + ofc], None, OP.add),
                                inc=sD, waits=[(sP, last), stq_free[b]])
                            stq_free[b] = self.dma(
                                ph, "gpsimd", f"p1as{b}",
                                d["kaT"][ofc * 128:(ofc + 1) * 128, gi * 512:(gi + 1) * 512],
                                stb[:], waits=[(sD, cv2)])
                        bankg[b] = cv2
                vev = 0
                for ti in range(4):
                    for dc in range(8):
                        w = []
                        if dc == 0:
                            w = [(sD, lnT_ready), (sD, vev)]
                        ph.op("tensor", lambda e, ti=ti, dc=dc, lt=lt: e.matmul(
                            psv[:, 0:512], lt[:, dc * 512 + ti * 128:dc * 512 + (ti + 1) * 128],
                            wva[:, dc * 1024:dc * 1024 + 512],
                            start=(dc == 0), stop=(dc == 7)), inc=sP, waits=w)
                        last = ph.op("tensor", lambda e, ti=ti, dc=dc, lt=lt: e.matmul(
                            psv[:, 512:1024], lt[:, dc * 512 + ti * 128:dc * 512 + (ti + 1) * 128],
                            wva[:, dc * 1024 + 512:(dc + 1) * 1024],
                            start=(dc == 0), stop=(dc == 7)), inc=sP)
                    vev = ph.op("vector", lambda e, gi=gi, ti=ti: e.tensor_tensor(
                        vA[:, (gi * 4 + ti) * 1024:(gi * 4 + ti + 1) * 1024],
                        psv[:], bva_rep[:], OP.add),
                        inc=sD, waits=[(sP, last)])
                lnT_free[gi % 2] = last
            ph.realize()

    def p1b(self):
        nc, d, g = self.nc, self.d, self.g
        ph = Phase(self, "p1b")
        sP, sD, sA = self.sP, self.sD, self.sA
        vA = self.vA

        with ExitStack() as es:
            A = es.enter_context
            woa = A(nc.sbuf_tensor("p1b_woa", [128, 8 * DM], BF16))
            maskmul = A(nc.sbuf_tensor("p1b_mm", [128, 128], F32))
            maskmul1 = A(nc.sbuf_tensor("p1b_mm1", [128, 128], F32))
            maskbias1 = A(nc.sbuf_tensor("p1b_mb1", [128, 128], F32))
            boa_rep = A(nc.sbuf_tensor("p1b_boa", [128, DM], F32))
            maskbias = A(nc.sbuf_tensor("p1b_mb", [128, 128], F32))
            c32 = A(nc.sbuf_tensor("p1b_c32", [128, 512], F32))
            qtc = A(nc.sbuf_tensor("p1b_qtc", [128, 4096], BF16))
            ksl = [A(nc.sbuf_tensor(f"p1b_ksl{i}", [128, 4096], BF16)) for i in range(2)]
            pT = A(nc.sbuf_tensor("p1b_pT", [128, 32 * 512], BF16))
            pstage = [A(nc.sbuf_tensor(f"p1b_ps{i}", [128, 512], BF16)) for i in range(2)]
            oT = A(nc.sbuf_tensor("p1b_oT", [128, 8 * 512], BF16))
            dacc = A(nc.sbuf_tensor("p1b_dacc", [128, 32], F32))
            den = A(nc.sbuf_tensor("p1b_den", [128, 1], F32))
            recden = A(nc.sbuf_tensor("p1b_rd", [128, 4], F32))
            m7 = A(nc.sbuf_tensor("p1b_m7", [128, 1], F32))
            nbias = A(nc.sbuf_tensor("p1b_nbias", [128, 1], F32))
            vcol = A(nc.sbuf_tensor("p1b_vcol", [128, 1], BF16))
            pvst = A(nc.sbuf_tensor("p1b_pvst", [1, 128], BF16))
            vrep = A(nc.sbuf_tensor("p1b_vrep", [128, 128], BF16))
            xin = A(nc.sbuf_tensor("p1b_xin", [128, 1024], F32))
            xo = A(nc.sbuf_tensor("p1b_xo", [128, 1024], F32))
            psS = [A(nc.psum_tensor(f"p1b_psS{i}", [128, 512], F32)) for i in range(2)]
            psT = [A(nc.psum_tensor(f"p1b_psT{i}", [128, 1024], BF16)) for i in range(2)]
            psPV = [A(nc.psum_tensor(f"p1b_psPV{i}", [128, 512], F32)) for i in range(2)]
            psY = [A(nc.psum_tensor(f"p1b_psY{i}", [128, 512], F32)) for i in range(2)]

            wld = self.dma(ph, "gpsimd", "p1bw", woa[:].rearrange("p (fc o) -> p fc o", fc=8),
                           d["a_wo"].rearrange("(fc p) o -> p fc o", p=128))
            self.dma(ph, "sync", "p1bw2", boa_rep[:], d["c_boa"])
            self.dma(ph, "sync", "p1bw2", maskmul[:], d["c_mm"])
            self.dma(ph, "sync", "p1bw2", maskmul1[:], d["c_mm1"])
            self.dma(ph, "sync", "p1bw2", maskbias1[:], d["c_mb1"])
            bld = self.dma(ph, "sync", "p1bw2", maskbias[:], d["c_mb"])
            ph.wait("tensor", wld)
            ph.wait("vector", bld)
            cz = ph.op("vector", lambda e: e.memset(c32[:], MASKVAL), inc=sD)

            qtc_free = 0
            ksl_free = [0, 0]
            psS_free = [0, 0]
            psT_free = [0, 0]
            pstage_free = [0, 0]
            pT_free = 0
            oT_free = 0
            psPV_free = [0, 0]
            psY_free = [0, 0]
            xo_free = (sD, 0)
            xin_free = 0

            for qc in range(8):
                qld = self.dma(ph, "sync", "p1bq", qtc[:].rearrange("p (dc t) -> p dc t", dc=8),
                               d["qaT"].rearrange("(dc p) t -> p dc t", p=128)
                               [:, :, qc * 512:(qc + 1) * 512],
                               waits=[(sP, qtc_free)])
                if qc == 7:
                    zm = ph.op("vector", lambda e: e.memset(pT[:, 0:28 * 512], 0.0),
                               inc=sD, waits=[(sP, pT_free)])
                    ph.wait("tensor", sD, zm)
                expn = [0, 0, 0, 0]
                dacc_free = getattr(self, "_p1b_dacc_free", 0)
                for kc in range(qc, 8):
                    kb2 = (kc - qc) % 2
                    kld = self.dma(ph, "sync", f"p1bk{kb2}",
                                   ksl[kb2][:].rearrange("p (dc t) -> p dc t", dc=8),
                                   d["kaT"].rearrange("(dc p) t -> p dc t", p=128)
                                   [:, :, kc * 512:(kc + 1) * 512],
                                   waits=[(sP, ksl_free[kb2])])
                    for qt in range(4):
                        qi = qc * 4 + qt
                        sb2 = (kc * 4 + qt) % 2
                        bank = psS[sb2]
                        last = 0
                        for dc in range(8):
                            w = []
                            if dc == 0:
                                w = [kld, (sA, psS_free[sb2])]
                                if kc == qc and qt == 0:
                                    w.append(qld)
                            last = ph.op("tensor", lambda e, dc=dc, qt=qt, kc=kc, bank=bank, kb2=kb2:
                                         e.matmul(bank[:],
                                                  qtc[:, dc * 512 + qt * 128:dc * 512 + (qt + 1) * 128],
                                                  ksl[kb2][:, dc * 512:(dc + 1) * 512],
                                                  start=(dc == 0), stop=(dc == 7)),
                                         inc=sP, waits=w)
                        mk = 0
                        first_mask = True
                        for kk in range(4):
                            kbg = kc * 4 + kk
                            w = [(sP, last)] if first_mask else ()
                            if kbg < qi:
                                mk = ph.op("vector", lambda e, bank=bank, kk=kk: e.memset(
                                    bank[:, kk * 128:(kk + 1) * 128], MASKVAL), inc=sD, waits=w)
                                first_mask = False
                            elif kbg == qi:
                                ph.op("vector", lambda e, bank=bank, kk=kk: e.tensor_tensor(
                                    bank[:, kk * 128:(kk + 1) * 128],
                                    bank[:, kk * 128:(kk + 1) * 128], maskmul[:], OP.mult),
                                    waits=w)
                                mk = ph.op("vector", lambda e, bank=bank, kk=kk: e.tensor_tensor(
                                    bank[:, kk * 128:(kk + 1) * 128],
                                    bank[:, kk * 128:(kk + 1) * 128], maskbias[:], OP.add),
                                    inc=sD)
                                first_mask = False
                            elif kbg == qi + 1:
                                ph.op("vector", lambda e, bank=bank, kk=kk: e.tensor_tensor(
                                    bank[:, kk * 128:(kk + 1) * 128],
                                    bank[:, kk * 128:(kk + 1) * 128], maskmul1[:], OP.mult),
                                    waits=w)
                                mk = ph.op("vector", lambda e, bank=bank, kk=kk: e.tensor_tensor(
                                    bank[:, kk * 128:(kk + 1) * 128],
                                    bank[:, kk * 128:(kk + 1) * 128], maskbias1[:], OP.add),
                                    inc=sD)
                                first_mask = False
                        expw = [(sP, last)] + ([(sD, mk)] if mk else [])
                        if kc == qc and qt == 0:
                            expw.append((sD, dacc_free))
                        use_bias = (qi == 31 and kc == 7)
                        if use_bias:
                            mx = ph.op("vector", lambda e, bank=bank: e.tensor_reduce(
                                m7[:], bank[:], AXX, OP.max), inc=sD,
                                waits=[(sP, last)] + ([(sD, mk)] if mk else []))
                            self.dr(ph)
                            nbv = ph.op("vector", lambda e: e.tensor_scalar(
                                nbias[:], m7[:], -1.0 / 32.0, None, OP.mult), inc=sD)
                            self.dr(ph)
                            expw.append((sD, nbv))
                        pb = (kc * 4 + qt) % 2
                        col = qt * 8 + kc
                        if use_bias:
                            ex = ph.op("scalar", lambda e, bank=bank, pb=pb, col=col: e.activation(
                                pstage[pb][:], bank[:], AF.Exp, bias=nbias[:], scale=1.0 / 32.0,
                                accum_out=dacc[:, col:col + 1]),
                                inc=sA, waits=expw + [(sP, pstage_free[pb])])
                        else:
                            ex = ph.op("scalar", lambda e, bank=bank, pb=pb, col=col: e.activation(
                                pstage[pb][:], bank[:], AF.Exp, scale=1.0 / 32.0,
                                accum_out=dacc[:, col:col + 1]),
                                inc=sA, waits=expw + [(sP, pstage_free[pb])])
                        psS_free[sb2] = ex
                        expn[qt] = ex
                        tb = (kc * 4 + qt) % 2
                        tp = 0
                        for j in range(4):
                            w = [(sA, ex), (sD, psT_free[tb])] if j == 0 else ()
                            tp = ph.op("tensor", lambda e, j=j, pb=pb, tb=tb: e.transpose(
                                psT[tb][:, j * 128:(j + 1) * 128],
                                pstage[pb][:, j * 128:(j + 1) * 128], g["identbf"][:]),
                                inc=sP, waits=w)
                        pstage_free[pb] = tp
                        evw = [(sP, tp)]
                        if qc == 7 and kc == 7 and qt == 0:
                            pass
                        cv = ph.op("vector", lambda e, tb=tb, kc=kc, qt=qt: e.tensor_copy(
                            pT.rearrange("p (kb c) -> p kb c", kb=32)
                            [:, kc * 4:(kc + 1) * 4, qt * 128:(qt + 1) * 128],
                            psT[tb][:, 0:512].rearrange("p (kb c) -> p kb c", kb=4)),
                            inc=sD, waits=evw)
                        psT_free[tb] = cv
                    ksl_free[kb2] = last
                qtc_free = last
                pT_ready_d = cv

                if qc == 7:
                    av31 = ph.op("scalar", lambda e: e.activation(
                        vcol[:], c32[:, 0:1], AF.Exp, bias=nbias[:], scale=1.0 / 32.0),
                        inc=sA)
                    tp = ph.op("tensor", lambda e: e.transpose(
                        psT[0][0:1, 0:128], vcol[:], g["identbf"][:]), inc=sP,
                        waits=[(sA, av31), (sD, psT_free[0])])
                    cv2 = ph.op("vector", lambda e: e.tensor_copy(pvst[:], psT[0][0:1, 0:128]),
                                inc=sD, waits=[(sP, tp)])
                    psT_free[0] = cv2
                    tpv = ph.op("tensor", lambda e: e.matmul(
                        psY[0][:, 0:128], g["ones1b"][:], pvst[:], start=True, stop=True),
                        inc=sP, waits=[(sD, cv2), (sD, psY_free[0])])
                    cvv = ph.op("vector", lambda e: e.tensor_copy(vrep[:], psY[0][:, 0:128]),
                                inc=sD, waits=[(sP, tpv)])
                    psY_free[0] = cvv
                    for kb in range(28):
                        cv3 = ph.op("vector", lambda e, kb=kb: e.tensor_copy(
                            pT[:, kb * 512 + 384:(kb + 1) * 512], vrep[:]), inc=sD)
                    pT_ready_d = cv3

                for qt in range(4):
                    qi = qc * 4 + qt
                    dw = [(sA, expn[qt])]
                    dn = ph.op("vector", lambda e, qt=qt, qc=qc: e.tensor_reduce(
                        den[:], dacc[:, qt * 8 + qc:qt * 8 + 8], AXX, OP.add), inc=sD, waits=dw)
                    self.dr(ph)
                    if qi == 31:
                        self.dr(ph)
                        ph.op("vector", lambda e: e.tensor_scalar(
                            m7[:], vcol[:], 3584.0, None, OP.mult), waits=[(sA, av31)])
                        self.dr(ph)
                        dn = ph.op("vector", lambda e: e.tensor_tensor(den[:], den[:], m7[:], OP.add),
                                   inc=sD)
                        self.dr(ph)
                    lq_ = ph.op("scalar", lambda e: e.activation(m7[:], den[:], AF.Ln),
                                inc=sA, waits=[(sD, dn)])
                    lq_ = ph.op("scalar", lambda e, qt=qt: e.activation(
                        recden[:, qt:qt + 1], m7[:], AF.Exp, scale=-1.0), inc=sA)
                    ph.wait("vector", sA, lq_)
                    self.nr_recip(ph, recden[:, qt:qt + 1], den[:], m7[:])
                    dn = ph.op("vector", lambda e: e.engine_nop(), inc=sD)
                self._p1b_dacc_free = dn

                kbmin = 4 * qc if qc < 7 else 0
                lastpv = 0
                for fc in range(8):
                    b = fc % 2
                    bank = psPV[b]
                    for i, kb in enumerate(range(kbmin, 32)):
                        w = []
                        if i == 0:
                            w = [(sD, pT_ready_d), (sD, psPV_free[b])]
                        lastpv = ph.op("tensor", lambda e, kb=kb, fc=fc, bank=bank, kbmin=kbmin:
                                       e.matmul(bank[:],
                                                vA[:, kb * 1024 + fc * 128:kb * 1024 + (fc + 1) * 128],
                                                pT[:, kb * 512:(kb + 1) * 512],
                                                start=(kb == kbmin), stop=(kb == 31)),
                                       inc=sP, waits=w)
                    psPV_free[b] = ph.op("vector", lambda e, fc=fc, bank=bank: e.tensor_copy(
                        oT[:, fc * 512:(fc + 1) * 512], bank[:]),
                        inc=sD, waits=[(sP, lastpv), (sP, oT_free) if fc == 0 else (sP, 0)])
                pT_free = lastpv
                oT_ready = psPV_free[(8 - 1) % 2]

                lasty = 0
                for tl in range(4):
                    xld = self.dma(ph, "sync", "p1bx", xin[:],
                                   d["x"][(qc * 4 + tl) * 128:(qc * 4 + tl + 1) * 128, :],
                                   waits=[(sD, xin_free)])
                    for fc in range(8):
                        w = []
                        if fc == 0:
                            w = [(sD, oT_ready), (sD, psY_free[0]), (sD, psY_free[1])]
                        ph.op("tensor", lambda e, fc=fc, tl=tl: e.matmul(
                            psY[0][:], oT[:, fc * 512 + tl * 128:fc * 512 + (tl + 1) * 128],
                            woa[:, fc * 1024:fc * 1024 + 512],
                            start=(fc == 0), stop=(fc == 7)), inc=sP, waits=w)
                        lasty = ph.op("tensor", lambda e, fc=fc, tl=tl: e.matmul(
                            psY[1][:], oT[:, fc * 512 + tl * 128:fc * 512 + (tl + 1) * 128],
                            woa[:, fc * 1024 + 512:(fc + 1) * 1024],
                            start=(fc == 0), stop=(fc == 7)), inc=sP)
                    ev = 0
                    for eb in range(2):
                        ph.op("vector", lambda e, eb=eb, tl=tl: e.scalar_tensor_tensor(
                            xo[:, eb * 512:(eb + 1) * 512], psY[eb][:], recden[:, tl:tl + 1],
                            boa_rep[:, eb * 512:(eb + 1) * 512], OP.mult, OP.add),
                            waits=[(sP, lasty), xld, xo_free] if eb == 0 else ())
                        ev = ph.op("vector", lambda e, eb=eb: e.tensor_tensor(
                            xo[:, eb * 512:(eb + 1) * 512], xo[:, eb * 512:(eb + 1) * 512],
                            xin[:, eb * 512:(eb + 1) * 512], OP.add), inc=sD)
                    psY_free[0] = ev
                    psY_free[1] = ev
                    xin_free = ev
                    xo_free = self.dma(ph, "gpsimd", "p1bxo",
                                       d["x1s"][(qc * 4 + tl) * 128:(qc * 4 + tl + 1) * 128, :],
                                       xo[:], waits=[(sD, ev)])
                oT_free = lasty
            ph.realize()

    def ffn(self, name, xsrc, xdst):
        nc, d, g = self.nc, self.d, self.g
        sP, sD, sA = self.sP, self.sD, self.sA
        es_ffn = ExitStack()
        srowbuf = es_ffn.enter_context(nc.sbuf_tensor(name + "_srow", [1, S + 2], F32))

        ph = Phase(self, name + "a")
        with ExitStack() as es:
            A = es.enter_context
            sx = [A(nc.sbuf_tensor(f"{name}a_x{i}", [128, 4096], F32)) for i in range(2)]
            scr = A(nc.sbuf_tensor(f"{name}a_scr", [128, 1024], F32))
            i32 = A(nc.sbuf_tensor(f"{name}a_i32", [128, 1024], I32))
            xq = [A(nc.sbuf_tensor(f"{name}a_xq{i}", [128, 1024], BF16)) for i in range(2)]
            xqT = [A(nc.sbuf_tensor(f"{name}a_xqT{i}", [128, 4096], BF16)) for i in range(2)]
            S4 = A(nc.sbuf_tensor(f"{name}a_S4", [128, 36], F32))
            rr4 = A(nc.sbuf_tensor(f"{name}a_rr4", [128, 4], F32))
            nb4 = A(nc.sbuf_tensor(f"{name}a_nb4", [128, 4], F32))
            rs4 = A(nc.sbuf_tensor(f"{name}a_rs4", [128, 4], F32))
            nbs4 = A(nc.sbuf_tensor(f"{name}a_nbs4", [128, 4], F32))
            ivc = A(nc.sbuf_tensor(f"{name}a_ivc", [128, 4], F32))
            pst = [A(nc.psum_tensor(f"{name}a_pst{i}", [128, 1024], BF16)) for i in range(2)]
            psr = A(nc.psum_tensor(f"{name}a_psr", [128, 512], F32))

            ph.op("vector", lambda e: e.memset(srowbuf[0:1, 0:1], 1.0))
            ph.op("vector", lambda e: e.memset(srowbuf[0:1, S + 1:S + 2], 1.0))
            pst_free = [0, 0]
            xq_free = [0, 0]
            stats_free = [0]
            x_free = [0, 0]
            xqT_free = [(sD, 0), (sD, 0)]
            psr_free = 0
            for gi in range(8):
                xb = sx[gi % 2]
                ld = self.dma(ph, "sync", f"{name}ax{gi % 2}",
                              xb[:].rearrange("p (t c) -> p t c", t=4),
                              d[xsrc][gi * 512:(gi + 1) * 512, :].rearrange("(t p) c -> p t c", p=128),
                              waits=[(sA, x_free[gi % 2])])
                cvT = 0
                sv = self.ln_stats_group(
                    ph, [xb[:, ti * 1024:(ti + 1) * 1024] for ti in range(4)],
                    scr[:], S4[:], rr4[:], nb4[:], [[ld]] * 4,
                    free_wait=[(sA, stats_free[0])],
                    quant=(rs4[:], nbs4[:], ivc[:]))
                for ti in range(4):
                    b = ti % 2
                    tile = xb[:, ti * 1024:(ti + 1) * 1024]
                    tpr = ph.op("tensor", lambda e, ti=ti: e.transpose(
                        psr[0:1, ti * 128:(ti + 1) * 128], ivc[:, ti:ti + 1], g["identf"][:]),
                        inc=sP, waits=[(sD, sv), (sD, psr_free)])
                    psr_free = ph.op("vector", lambda e, gi=gi, ti=ti: e.tensor_copy(
                        srowbuf[0:1, 1 + (gi * 4 + ti) * 128:1 + (gi * 4 + ti + 1) * 128],
                        psr[0:1, ti * 128:(ti + 1) * 128]), inc=sD, waits=[(sP, tpr)])
                    av = ph.op("scalar", lambda e, t=tile, ti=ti: e.activation(
                        i32[:], t, AF.Identity, bias=nbs4[:, ti:ti + 1], scale=rs4[:, ti:ti + 1]),
                        inc=sA, waits=[(sD, sv), (sD, getattr(self, "_fa_i32free", 0))])
                    stats_free[0] = av
                    if ti == 3:
                        x_free[gi % 2] = av
                    cv = ph.op("vector", lambda e, b=b: e.tensor_copy(xq[b][:], i32[:]),
                               inc=sD, waits=[(sA, av), (sP, xq_free[b])])
                    self._fa_i32free = cv
                    tp = 0
                    for j in range(8):
                        w = [(sD, cv), (sD, pst_free[b])] if j == 0 else ()
                        tp = ph.op("tensor", lambda e, j=j, b=b: e.transpose(
                            pst[b][:, j * 128:(j + 1) * 128], xq[b][:, j * 128:(j + 1) * 128],
                            g["identbf"][:]), inc=sP, waits=w)
                    xq_free[b] = tp
                    cvT = ph.op("vector", lambda e, b=b, xt=xqT[gi % 2], ti=ti: e.tensor_copy(
                        xt.rearrange("p (j c) -> p j c", j=8)[:, :, ti * 128:(ti + 1) * 128],
                        pst[b].rearrange("p (j c) -> p j c", j=8)),
                        inc=sD, waits=[(sP, tp)] + ([xqT_free[gi % 2]] if ti == 0 else []))
                    pst_free[b] = cvT
                stv = None
                for dc in range(8):
                    stv = self.dma(ph, "gpsimd", f"{name}aq{gi % 2}",
                                   d["xqTs"][dc * 128:(dc + 1) * 128, gi * 512:(gi + 1) * 512],
                                   xqT[gi % 2].rearrange("p (j c) -> p j c", j=8)[:, dc, :],
                                   waits=[(sD, cvT)] if dc == 0 else ())
                xqT_free[gi % 2] = stv
            ph.realize()

        ph = Phase(self, name + "b")
        with ExitStack() as es:
            A = es.enter_context
            w1 = A(nc.sbuf_tensor(f"{name}b_w1", [128, 8 * DH], F8))
            w2 = A(nc.sbuf_tensor(f"{name}b_w2", [128, 32 * DM], F8))
            xqs = [A(nc.sbuf_tensor(f"{name}b_xq0", [128, 8 * 258], BF16))]
            srep = [A(nc.sbuf_tensor(f"{name}b_sr0", [128, 258], F32))]
            s2rep = [A(nc.sbuf_tensor(f"{name}b_s2r{i}", [128, 256], F32)) for i in range(2)]
            hpre = A(nc.sbuf_tensor(f"{name}b_hpre", [128, 32 * 258], F32))
            hpost = A(nc.sbuf_tensor(f"{name}b_hpost", [128, 32 * 256], F32))
            hq = A(nc.sbuf_tensor(f"{name}b_hq", [128, 32 * 256], BF16))
            hi32 = A(nc.sbuf_tensor(f"{name}b_hi32", [128, 256], I32))
            cmax = A(nc.sbuf_tensor(f"{name}b_cmax", [128, 256], F32))
            cmin = A(nc.sbuf_tensor(f"{name}b_cmin", [128, 256], F32))
            camx = A(nc.sbuf_tensor(f"{name}b_camx", [128, 256], F32))
            mx2 = A(nc.sbuf_tensor(f"{name}b_mx2", [128, 2], F32))
            ssc = A(nc.sbuf_tensor(f"{name}b_ssc", [128, 2], F32))
            s2c = A(nc.sbuf_tensor(f"{name}b_s2c", [128, 2], F32))
            s2row_sb = A(nc.sbuf_tensor(f"{name}b_s2row", [1, 256], F32))
            t2c = A(nc.sbuf_tensor(f"{name}b_t2c", [128, 2], F32))
            xin = [A(nc.sbuf_tensor(f"{name}b_xin{i}", [128, 1024], F32)) for i in range(2)]
            xo = A(nc.sbuf_tensor(f"{name}b_xo", [128, 1024], F32))
            psm1 = [A(nc.psum_tensor(f"{name}b_psm1{i}", [128, 512], F32)) for i in range(2)]
            psm2 = [A(nc.psum_tensor(f"{name}b_psm2{i}", [128, 512], F32)) for i in range(2)]
            pstr = A(nc.psum_tensor(f"{name}b_pstr", [128, 512], F32))
            psrep = A(nc.psum_tensor(f"{name}b_psrep", [128, 512], F32))

            wl1 = self.dma(ph, "sync", f"{name}bw", w1[:].rearrange("p (dc h) -> p dc h", dc=8),
                           d["w1t8"].rearrange("(dc p) h -> p dc h", p=128))
            wl2 = self.dma(ph, "sync", f"{name}bw", w2[:].rearrange("p (hc e) -> p hc e", hc=32),
                           d["w2t8"].rearrange("(hc p) e -> p hc e", p=128))
            ph.wait("tensor", wl2)

            xq_free = [0, 0]
            srep_free = [0, 0]
            psrep_free = 0
            s2_free = [0, 0]
            psm1_free = [0, 0]
            psm2_free = [0, 0]
            hq_free = 0
            xin_free = [0, 0]
            xo_free = (sD, 0)
            for c in range(16):
                cb = 0
                t0 = c * 256
                lo = max(0, t0 - 1)
                hi = min(S, t0 + 257)
                off = lo - (t0 - 1)
                ln = hi - lo
                xld = self.dma(ph, "sync", f"{name}bx{cb}",
                               xqs[cb].rearrange("p (j c) -> p j c", j=8)[:, :, off:off + ln],
                               d["xqTs"].rearrange("(dc p) t -> p dc t", p=128)[:, :, lo:hi],
                               waits=[(sP, xq_free[cb])])
                tpr = ph.op("tensor", lambda e, t0=t0: e.matmul(
                    psrep[:, 0:258], g["ones1f"][:], srowbuf[0:1, t0:t0 + 258],
                    start=True, stop=True), inc=sP,
                    waits=[(sD, srep_free[cb]), (sD, psrep_free)])
                sld = ph.op("vector", lambda e, cb=cb: e.tensor_copy(srep[cb][:], psrep[:, 0:258]),
                            inc=sD, waits=[(sP, tpr)])
                psrep_free = sld
                zs = None
                if c == 0:
                    for j in range(8):
                        zs = ph.op("vector", lambda e, cb=cb, j=j: e.memset(
                            xqs[cb].rearrange("p (j c) -> p j c", j=8)[:, j, 0:1], 0.0), inc=sD)
                if c == 15:
                    for j in range(8):
                        zs = ph.op("vector", lambda e, cb=cb, j=j: e.memset(
                            xqs[cb].rearrange("p (j c) -> p j c", j=8)[:, j, 257:258], 0.0), inc=sD,
                            waits=[(sP, xq_free[cb])] if j == 0 else ())
                xr = xqs[cb].rearrange("p (j c) -> p j c", j=8)

                xl = [None, None]
                for tl in range(2):
                    xl[tl] = self.dma(ph, "sync", f"{name}br{tl}", xin[tl][:],
                                      d[xsrc][t0 + tl * 128:t0 + (tl + 1) * 128, :],
                                      waits=[(sD, xin_free[tl])])

                lastm1 = 0
                tree = 0
                for hc in range(32):
                    pb = hc % 2
                    bank = psm1[pb]
                    for dc in range(8):
                        w = []
                        if dc == 0:
                            w = [xld, (sD, psm1_free[pb])]
                            if hc == 0 and zs is not None:
                                w.append((sD, zs))
                        lastm1 = ph.op("tensor", lambda e, hc=hc, dc=dc, bank=bank, xr=xr:
                                       e.matmul(bank[:, 0:258],
                                                w1[:, dc * 4096 + hc * 128:dc * 4096 + (hc + 1) * 128],
                                                xr[:, dc, :], start=(dc == 0), stop=(dc == 7)),
                                       inc=sP, waits=w)
                    hslice = hpre[:, hc * 258:(hc + 1) * 258]
                    w = [(sP, lastm1), (sD, sld)]
                    ev = ph.op("vector", lambda e, bank=bank, hs=hslice, cb=cb: e.tensor_tensor(
                        hs, bank[:, 0:258], srep[cb][:], OP.mult), inc=sD, waits=w)
                    psm1_free[pb] = ev
                    hp = hpost[:, hc * 256:(hc + 1) * 256]
                    ph.op("vector", lambda e, hs=hslice, hp=hp, hc=hc: e.tensor_scalar(
                        hp, hs[:, 0:256], g["convw"][:, hc * 3:hc * 3 + 1],
                        g["convb"][:, hc:hc + 1], OP.mult, OP.add))
                    ph.op("vector", lambda e, hs=hslice, hp=hp, hc=hc: e.scalar_tensor_tensor(
                        hp, hs[:, 1:257], g["convw"][:, hc * 3 + 1:hc * 3 + 2], hp,
                        OP.mult, OP.add))
                    cva = ph.op("vector", lambda e, hs=hslice, hp=hp, hc=hc: e.scalar_tensor_tensor(
                        hp, hs[:, 2:258], g["convw"][:, hc * 3 + 2:hc * 3 + 3], hp,
                        OP.mult, OP.add), inc=sD)
                    sil = ph.op("scalar", lambda e, hp=hp: e.activation(hp, hp, AF.Silu),
                                inc=sA, waits=[(sD, cva)])
                    if hc == 0:
                        ph.op("vector", lambda e, hp=hp: e.tensor_copy(cmax[:], hp),
                              waits=[(sA, sil)])
                        tree = ph.op("vector", lambda e, hp=hp: e.tensor_copy(cmin[:], hp), inc=sD)
                    else:
                        ph.op("vector", lambda e, hp=hp: e.tensor_tensor(cmax[:], cmax[:], hp, OP.max),
                              waits=[(sA, sil)])
                        tree = ph.op("vector", lambda e, hp=hp: e.tensor_tensor(
                            cmin[:], cmin[:], hp, OP.min), inc=sD)

                am = ph.op("vector", lambda e: e.scalar_tensor_tensor(
                    camx[:], cmin[:], -1.0, cmax[:], OP.mult, OP.max), inc=sD, waits=[(sD, tree)])
                tp = 0
                for half in range(2):
                    tp = ph.op("tensor", lambda e, half=half: e.transpose(
                        pstr[:, half * 128:(half + 1) * 128], camx[:, half * 128:(half + 1) * 128],
                        g["identf"][:]), inc=sP, waits=[(sD, am)] if half == 0 else ())
                mxv = ph.op("vector", lambda e: e.tensor_reduce(
                    mx2[:, 0:1], pstr[:, 0:128], AXX, OP.max), inc=sD, waits=[(sP, tp)])
                mxv = ph.op("vector", lambda e: e.tensor_reduce(
                    mx2[:, 1:2], pstr[:, 128:256], AXX, OP.max), inc=sD)
                self.dr(ph)
                ph.op("vector", lambda e: e.tensor_scalar(mx2[:], mx2[:], 1e-5, None, OP.max))
                self.dr(ph)
                ph.op("vector", lambda e: e.tensor_scalar(
                    ssc[:], mx2[:], g["mw2"][:], None, OP.mult))
                self.dr(ph)
                mxv = ph.op("vector", lambda e: e.tensor_scalar(
                    ssc[:], ssc[:], 1.0 / 127.0, None, OP.mult), inc=sD)
                self.dr(ph)
                lq_ = ph.op("scalar", lambda e: e.activation(t2c[:], mx2[:], AF.Ln),
                            inc=sA, waits=[(sD, mxv)])
                lq_ = ph.op("scalar", lambda e: e.activation(s2c[:], t2c[:], AF.Exp, scale=-1.0),
                            inc=sA)
                ph.wait("vector", sA, lq_)
                self.nr_recip(ph, s2c[:], mx2[:], t2c[:])
                mxv = ph.op("vector", lambda e: e.tensor_scalar(
                    s2c[:], s2c[:], 127.0, None, OP.mult), inc=sD)
                self.dr(ph)
                tpr = 0
                for tl in range(2):
                    tpr = ph.op("tensor", lambda e, tl=tl: e.transpose(
                        pstr[0:1, 256 + tl * 128:256 + (tl + 1) * 128], s2c[:, tl:tl + 1],
                        g["identf"][:]), inc=sP, waits=[(sD, mxv)] if tl == 0 else ())
                cvr = ph.op("vector", lambda e: e.tensor_copy(s2row_sb[:], pstr[0:1, 256:512]),
                            inc=sD, waits=[(sP, tpr)])
                tpr = ph.op("tensor", lambda e: e.matmul(pstr[:, 256:512], g["ones1f"][:],
                                                         s2row_sb[:], start=True, stop=True),
                            inc=sP, waits=[(sD, cvr)])
                s2ld = ph.op("vector", lambda e, cb=cb: e.tensor_copy(s2rep[cb][:], pstr[:, 256:512]),
                             inc=sD, waits=[(sP, tpr), (sD, s2_free[cb])])

                lasthq = 0
                for hc in range(32):
                    hp = hpost[:, hc * 256:(hc + 1) * 256]
                    qm = ph.op("vector", lambda e, hp=hp, cb=cb: e.tensor_tensor(
                        hi32[:], hp, s2rep[cb][:], OP.mult), inc=sD,
                        waits=[(sD, s2ld)] if hc == 0 else ())
                    lasthq = ph.op("vector", lambda e, hc=hc: e.tensor_copy(
                        hq[:, hc * 256:(hc + 1) * 256], hi32[:]), inc=sD,
                        waits=[(sP, hq_free)] if hc == 0 else ())
                s2_free[cb] = lasthq

                lastm2 = 0
                for tl in range(2):
                    for hc in range(32):
                        w = []
                        if hc == 0:
                            w = [(sD, lasthq), (sD, psm2_free[0]), (sD, psm2_free[1])]
                        ph.op("tensor", lambda e, hc=hc, tl=tl: e.matmul(
                            psm2[0][:], hq[:, hc * 256 + tl * 128:hc * 256 + (tl + 1) * 128],
                            w2[:, hc * 1024:hc * 1024 + 512],
                            start=(hc == 0), stop=(hc == 31)), inc=sP, waits=w)
                        lastm2 = ph.op("tensor", lambda e, hc=hc, tl=tl: e.matmul(
                            psm2[1][:], hq[:, hc * 256 + tl * 128:hc * 256 + (tl + 1) * 128],
                            w2[:, hc * 1024 + 512:(hc + 1) * 1024],
                            start=(hc == 0), stop=(hc == 31)), inc=sP)
                    ev = 0
                    for eb in range(2):
                        ev = ph.op("vector", lambda e, eb=eb, tl=tl: e.scalar_tensor_tensor(
                            xo[:, eb * 512:(eb + 1) * 512], psm2[eb][:], ssc[:, tl:tl + 1],
                            xin[tl][:, eb * 512:(eb + 1) * 512], OP.mult, OP.add), inc=sD,
                            waits=[(sP, lastm2), xl[tl], xo_free] if eb == 0 else ())
                    psm2_free[0] = ev
                    psm2_free[1] = ev
                    xin_free[tl] = ev
                    xo_free = self.dma(ph, "gpsimd", f"{name}bo",
                                       d[xdst][t0 + tl * 128:t0 + (tl + 1) * 128, :],
                                       xo[:], waits=[(sD, ev)])
                hq_free = lastm2
                xq_free[cb] = lastm1
                srep_free[cb] = ev
            ph.realize()

    def p3a(self):
        nc, d, g = self.nc, self.d, self.g
        ph = Phase(self, "p3a")
        sP, sD, sA = self.sP, self.sD, self.sA

        with ExitStack() as es:
            A = es.enter_context
            wq = A(nc.sbuf_tensor("p3a_wq", [128, 8 * DM], BF16))
            wk = A(nc.sbuf_tensor("p3a_wk", [128, 8 * DM], BF16))
            wv = A(nc.sbuf_tensor("p3a_wv", [128, 8 * DM], BF16))
            sx = [A(nc.sbuf_tensor(f"p3a_x{i}", [128, 4096], F32)) for i in range(2)]
            scr = A(nc.sbuf_tensor("p3a_scr", [128, 1024], F32))
            sln = [A(nc.sbuf_tensor(f"p3a_ln{i}", [128, 1024], BF16)) for i in range(2)]
            lnT = [A(nc.sbuf_tensor(f"p3a_lnT{i}", [128, 4096], BF16)) for i in range(2)]
            S4 = A(nc.sbuf_tensor("p3a_S4", [128, 20], F32))
            rr4 = A(nc.sbuf_tensor("p3a_rr4", [128, 4], F32))
            nb4 = A(nc.sbuf_tensor("p3a_nb4", [128, 4], F32))
            stq = [A(nc.sbuf_tensor(f"p3a_stq{i}", [128, 512], BF16)) for i in range(2)]
            stv = [A(nc.sbuf_tensor(f"p3a_stv{i}", [128, 1040], BF16)) for i in range(2)]
            pst = [A(nc.psum_tensor(f"p3a_pst{i}", [128, 1024], BF16)) for i in range(2)]
            psq = [A(nc.psum_tensor(f"p3a_psq{i}", [128, 512], F32)) for i in range(2)]
            psv = A(nc.psum_tensor("p3a_psv", [128, 1024], F32))

            for src, dst in (("wqTb", wq), ("wkTb", wk), ("wvTb", wv)):
                wld = self.dma(ph, "sync", "p3aw", dst[:].rearrange("p (dc o) -> p dc o", dc=8),
                               d[src].rearrange("(dc p) o -> p dc o", p=128))
            ph.wait("tensor", wld)
            vset = 0
            for i in range(2):
                vset = ph.op("vector", lambda e, i=i: e.memset(
                    stv[i].rearrange("p (hk c) -> p hk c", c=65)[:, :, 64:65], 1.0), inc=sD)

            lnT_free = [0, 0]
            pst_free = [0, 0]
            sln_free = [0, 0]
            stq_free = [(sD, 0), (sD, 0)]
            stv_free = [(sD, 0), (sD, 0)]
            stats_free = [0]
            x_free = [0, 0]
            bankg = [0, 0]
            for gi in range(8):
                xb = sx[gi % 2]
                ld = self.dma(ph, "sync", f"p3ax{gi % 2}", xb[:].rearrange("p (t c) -> p t c", t=4),
                              d["x2s"][gi * 512:(gi + 1) * 512, :].rearrange("(t p) c -> p t c", p=128),
                              waits=[(sA, x_free[gi % 2])])
                cv = 0
                sv = self.ln_stats_group(
                    ph, [xb[:, ti * 1024:(ti + 1) * 1024] for ti in range(4)],
                    scr[:], S4[:], rr4[:], nb4[:], [[ld]] * 4,
                    free_wait=[(sA, stats_free[0])])
                for ti in range(4):
                    b = ti % 2
                    tile = xb[:, ti * 1024:(ti + 1) * 1024]
                    lnb = sln[b]
                    av = ph.op("scalar", lambda e, t=tile, o=lnb, ti=ti: e.activation(
                        o[:], t, AF.Identity, bias=nb4[:, ti:ti + 1], scale=rr4[:, ti:ti + 1]),
                        inc=sA, waits=[(sD, sv), (sP, sln_free[b])])
                    stats_free[0] = av
                    if ti == 3:
                        x_free[gi % 2] = av
                    bank = pst[b]
                    tp = 0
                    for j in range(8):
                        w = [(sA, av), (sD, pst_free[b])] if j == 0 else ()
                        tp = ph.op("tensor", lambda e, j=j, lb=lnb, bk=bank: e.transpose(
                            bk[:, j * 128:(j + 1) * 128], lb[:, j * 128:(j + 1) * 128],
                            g["identbf"][:]), inc=sP, waits=w)
                    sln_free[b] = tp
                    w2 = [(sP, tp)]
                    if ti == 0:
                        w2.append((sP, lnT_free[gi % 2]))
                    cv = ph.op("vector", lambda e, bk=bank, lt=lnT[gi % 2], ti=ti: e.tensor_copy(
                        lt.rearrange("p (j c) -> p j c", j=8)[:, :, ti * 128:(ti + 1) * 128],
                        bk.rearrange("p (j c) -> p j c", j=8)),
                        inc=sD, waits=w2)
                    pst_free[b] = cv
                lnT_ready = cv
                lt = lnT[gi % 2]
                last = 0
                for proj, wsb, dst in (("q", wq, "qbT"), ("k", wk, "kbT")):
                    for ofc in range(8):
                        b = ofc % 2
                        bank = psq[b]
                        for dc in range(8):
                            w = []
                            if dc == 0:
                                w = [(sD, lnT_ready), (sD, bankg[b])]
                            last = ph.op("tensor", lambda e, dc=dc, ofc=ofc, wsb=wsb, bk=bank, lt=lt:
                                         e.matmul(bk[:],
                                                  wsb[:, dc * 1024 + ofc * 128:dc * 1024 + (ofc + 1) * 128],
                                                  lt[:, dc * 512:(dc + 1) * 512],
                                                  start=(dc == 0), stop=(dc == 7)),
                                         inc=sP, waits=w)
                        stb = stq[b]
                        cv2 = ph.op("vector", lambda e, bk=bank, sb_=stb: e.tensor_copy(sb_[:], bk[:]),
                                    inc=sD, waits=[(sP, last), stq_free[b]])
                        stq_free[b] = self.dma(
                            ph, "gpsimd", f"p3as{b}",
                            d[dst][ofc * 128:(ofc + 1) * 128, gi * 512:(gi + 1) * 512],
                            stb[:], waits=[(sD, cv2)])
                        bankg[b] = cv2
                vev = 0
                for ti in range(4):
                    for dc in range(8):
                        w = []
                        if dc == 0:
                            w = [(sD, lnT_ready), (sD, vev)]
                        ph.op("tensor", lambda e, ti=ti, dc=dc, lt=lt: e.matmul(
                            psv[:, 0:512], lt[:, dc * 512 + ti * 128:dc * 512 + (ti + 1) * 128],
                            wv[:, dc * 1024:dc * 1024 + 512],
                            start=(dc == 0), stop=(dc == 7)), inc=sP, waits=w)
                        last = ph.op("tensor", lambda e, ti=ti, dc=dc, lt=lt: e.matmul(
                            psv[:, 512:1024], lt[:, dc * 512 + ti * 128:dc * 512 + (ti + 1) * 128],
                            wv[:, dc * 1024 + 512:(dc + 1) * 1024],
                            start=(dc == 0), stop=(dc == 7)), inc=sP)
                    b = ti % 2
                    vev = ph.op("vector", lambda e, b=b: e.tensor_copy(
                        stv[b].rearrange("p (hk c) -> p hk c", c=65)[:, :, 0:64],
                        psv[:].rearrange("p (hk dd) -> p hk dd", dd=64)),
                        inc=sD, waits=[(sP, last), stv_free[b],
                                       (sD, vset) if gi == 0 and ti < 2 else (sD, 0)])
                    stv_free[b] = self.dma(
                        ph, "gpsimd", f"p3av{b}",
                        d["vbp"][(gi * 4 + ti) * 128:(gi * 4 + ti + 1) * 128, 0:1040],
                        stv[b][:], waits=[(sD, vev)])
                lnT_free[gi % 2] = last
            ph.realize()

    def p3b(self):
        nc, d, g = self.nc, self.d, self.g
        sP, sD, sA = self.sP, self.sD, self.sA
        rT = self.es3.enter_context(nc.sbuf_tensor("p3b_rT", [128, 8 * 4096], BF16))

        ph = Phase(self, "p3b")
        with ExitStack() as es:
            A = es.enter_context
            qTb = [A(nc.sbuf_tensor(f"p3b_q{i}", [128, 2048], BF16)) for i in range(2)]
            kTb = [A(nc.sbuf_tensor(f"p3b_k{i}", [128, 2048], BF16)) for i in range(2)]
            pT = A(nc.sbuf_tensor("p3b_pT", [128, 32768], BF16))
            v3 = [A(nc.sbuf_tensor(f"p3b_v{i}", [128, 2080], BF16)) for i in range(2)]
            ostage = A(nc.sbuf_tensor("p3b_ost", [64, 512], BF16))
            dnst = A(nc.sbuf_tensor("p3b_dnst", [65, 2048], F32))
            t16 = A(nc.sbuf_tensor("p3b_t16", [128, 16], F32))
            rd16 = A(nc.sbuf_tensor("p3b_rd16", [128, 16], F32))
            rrow = A(nc.sbuf_tensor("p3b_rrow", [1, 512], F32))
            rep64 = A(nc.sbuf_tensor("p3b_rep64", [64, 512], F32))
            psS = [A(nc.psum_tensor(f"p3b_psS{i}", [128, 1024], F32)) for i in range(2)]
            psPV = [A(nc.psum_tensor(f"p3b_psPV{i}", [128, 512], F32)) for i in range(2)]
            psX1 = A(nc.psum_tensor("p3b_psX1", [128, 512], F32))
            psX2 = A(nc.psum_tensor("p3b_psX2", [128, 512], F32))

            exp_done = [0, 0]
            qk_free = [(sD, 0), (sD, 0)]
            expv = 0
            for w in range(32):
                wb = w % 2
                qld = 0
                for half in range(2):
                    self.dma(ph, "sync", f"p3bq{wb}",
                             qTb[wb][half * 64:(half + 1) * 64, :]
                             .rearrange("r (h t) -> r h t", h=16),
                             d["qbT"][:, w * 128:(w + 1) * 128]
                             .rearrange("(h r) t -> r h t", r=64),
                             waits=[qk_free[wb]] if half == 0 else ())
                    qld = self.dma(ph, "sync", f"p3bq{wb}",
                                   kTb[wb][half * 64:(half + 1) * 64, :]
                                   .rearrange("r (h t) -> r h t", h=16),
                                   d["kbT"][:, w * 128:(w + 1) * 128]
                                   .rearrange("(h r) t -> r h t", r=64))
                bank = psS[wb]
                last = 0
                for tl in range(128):
                    par = tl % 2
                    fo = (tl // 64) * 512 + ((tl // 2) % 32) * 16
                    w_ = []
                    if tl == 0:
                        w_ = [qld, (sA, exp_done[wb])]
                    last = ph.op("tensor", lambda e, tl=tl, par=par, fo=fo, wb=wb, bank=bank:
                                 e.matmul(bank[64 * par:64 * par + 16, fo:fo + 16],
                                          kTb[wb][64 * par:64 * par + 64, :]
                                          .rearrange("p (h t) -> p t h", t=128)[:, tl, :],
                                          qTb[wb][64 * par:64 * par + 64, :]
                                          .rearrange("p (h t) -> p t h", t=128)[:, tl, :],
                                          start=True, stop=True,
                                          tile_position=(64 * par, 64 * par)),
                                 inc=sP, waits=w_)
                for hb in range(2):
                    expv = ph.op("scalar", lambda e, w=w, hb=hb, bank=bank: e.activation(
                        pT[:, w * 1024 + hb * 512:w * 1024 + (hb + 1) * 512],
                        bank[:, hb * 512:(hb + 1) * 512], AF.Exp, scale=0.125),
                        inc=sA, waits=[(sP, last)] if hb == 0 else ())
                exp_done[wb] = expv
                qk_free[wb] = (sP, last)
            ph.wait("tensor", sA, expv)
            for wu in range(2):
                for js_ in (0, 64):
                    ph.op("tensor", lambda e, js_=js_: e.matmul(
                        psX1[0:65, 0:16], v3[0][js_:js_ + 16, 0:65],
                        pT[js_:js_ + 16, 0:16],
                        start=True, stop=True, tile_position=(js_, 0)), inc=sP)

            def pt_slice(t):
                return pT[64 * (t % 2):64 * (t % 2) + 16,
                          (t // 128) * 1024 + ((t % 128) // 64) * 512
                          + ((t // 2) % 32) * 16:
                          (t // 128) * 1024 + ((t % 128) // 64) * 512
                          + ((t // 2) % 32) * 16 + 16]

            psB = [psPV[0], psPV[1], psS[0][:, 0:512], psS[1][:, 0:512]]
            pv_free = [(sD, 0)] * 4
            v3_free = [(sP, 0), (sP, 0)]
            ost_free = (sD, 0)
            tp16_prev = 0
            ln_prev = 0
            tpb_prev = 0
            repev_prev = 0
            rt_evs = []
            grp = []
            bkn = 0
            for j in range(16):
                jc, jp, js = j // 2, j % 2, 64 * (j % 2)
                for ub in range(8):
                    pb = bkn % 4
                    vb = bkn % 2
                    t0 = 16 * 32 * ub + j
                    vld = self.dma(ph, "sync", f"p3bv{vb}",
                                   v3[vb][js:js + 16, :].rearrange("r (g c) -> r g c", c=65),
                                   d["vbp"][t0:t0 + 497:16, :]
                                   .rearrange("g (r c) -> r g c", c=65)[0:16],
                                   waits=[v3_free[vb]])
                    bank = psB[pb]
                    last = 0
                    for u_ in range(32):
                        t = 16 * (32 * ub + u_) + j
                        w_ = []
                        if u_ == 0:
                            w_ = [vld, pv_free[pb]]
                        last = ph.op("tensor", lambda e, u_=u_, t=t, js=js, pb=pb, vb=vb:
                                     e.matmul(psB[pb][0:65, u_ * 16:(u_ + 1) * 16],
                                              v3[vb][js:js + 16, u_ * 65:(u_ + 1) * 65],
                                              pt_slice(t),
                                              start=True, stop=True,
                                              tile_position=(js, 0)),
                                     inc=sP, waits=w_)
                    v3_free[vb] = (sP, last)
                    grp.append((pb, j, ub, last))
                    bkn += 1
                    if len(grp) < 4:
                        continue
                    cpv = 0
                    for i, (pbi, ji, ubi, pvl) in enumerate(grp):
                        cpv = ph.op("vector", lambda e, i=i, pbi=pbi: e.tensor_copy(
                            dnst[64:65, i * 512:(i + 1) * 512],
                            psB[pbi][64:65, 0:512]), inc=sD,
                            waits=[(sP, pvl)] + ([(sP, tp16_prev)] if i == 0 else []))
                    tp16 = 0
                    for x in range(16):
                        w_ = [(sD, cpv), (sA, ln_prev)] if x == 0 else ()
                        tp16 = ph.op("tensor", lambda e, x=x: e.transpose(
                            psX2[:, x:x + 1], dnst[64:65, x * 128:(x + 1) * 128],
                            g["identf"][64:65, 64:65]), inc=sP, waits=w_)
                    tp16_prev = tp16
                    ph.op("scalar", lambda e: e.activation(t16[:], psX2[:, 0:16], AF.Ln),
                          waits=[(sP, tp16), (sP, tpb_prev)])
                    lnv = ph.op("scalar", lambda e: e.activation(
                        rd16[:], t16[:], AF.Exp, scale=-1.0), inc=sA)
                    ln_prev = lnv
                    for i, (pbi, ji, ubi, pvl) in enumerate(grp):
                        jci, jpi = ji // 2, ji % 2
                        tpb = 0
                        for x in range(4):
                            w_ = [(sA, lnv), (sD, repev_prev)] if x == 0 else ()
                            tpb = ph.op("tensor", lambda e, i=i, x=x: e.transpose(
                                psX1[0:1, x * 128:(x + 1) * 128], rd16[:, i * 4 + x:i * 4 + x + 1],
                                g["identf"][:]), inc=sP, waits=w_)
                        rv = ph.op("vector", lambda e: e.tensor_copy(rrow[:], psX1[0:1, 0:512]),
                                   inc=sD, waits=[(sP, tpb)])
                        rm = ph.op("tensor", lambda e: e.matmul(
                            psX1[0:64, 0:512], g["ones1f"][:, 0:64], rrow[:],
                            start=True, stop=True), inc=sP, waits=[(sD, rv)])
                        tpb_prev = rm
                        rv = ph.op("vector", lambda e: e.tensor_copy(rep64[:], psX1[0:64, 0:512]),
                                   inc=sD, waits=[(sP, rm)])
                        repev_prev = rv
                        rtv = rT.rearrange("p (jj h u) -> p jj h u", jj=8, h=16)
                        if jpi == 0:
                            ev = ph.op("vector", lambda e, i=i, pbi=pbi, jci=jci, ubi=ubi:
                                       e.scalar_tensor_tensor(
                                           rtv[0:64, jci, :, ubi * 32:(ubi + 1) * 32],
                                           psB[pbi][0:64, 0:512].rearrange("p (u h) -> p h u", h=16),
                                           1.0,
                                           rep64[:].rearrange("p (u h) -> p h u", h=16),
                                           OP.mult, OP.mult), inc=sD, waits=[(sD, rv)])
                        else:
                            ev = ph.op("vector", lambda e, i=i, pbi=pbi: e.scalar_tensor_tensor(
                                ostage[:].rearrange("p (h u) -> p h u", h=16),
                                psB[pbi][0:64, 0:512].rearrange("p (u h) -> p h u", h=16),
                                1.0,
                                rep64[:].rearrange("p (u h) -> p h u", h=16),
                                OP.mult, OP.mult), inc=sD, waits=[(sD, rv), ost_free])
                            ost_free = self.dma(
                                ph, "gpsimd", "p3bo",
                                rtv[64:128, jci, :, ubi * 32:(ubi + 1) * 32],
                                ostage[:].rearrange("p (h u) -> p h u", h=16),
                                waits=[(sD, ev)])
                        pv_free[pbi] = (sD, ev)
                        rt_evs.append(ev)
                    grp = []
            ph.realize()

        ph = Phase(self, "p3bt")
        with ExitStack() as es:
            A = es.enter_context
            wosb = A(nc.sbuf_tensor("p3bt_wo", [128, 8 * DM], BF16))
            bob_rep = A(nc.sbuf_tensor("p3bt_bob", [128, DM], F32))
            xin = [A(nc.sbuf_tensor(f"p3bt_xi{i}", [128, 1024], F32)) for i in range(2)]
            xo = [A(nc.sbuf_tensor(f"p3bt_xo{i}", [128, 1024], F32)) for i in range(2)]
            psY = [A(nc.psum_tensor(f"p3bt_psY{i}", [128, 512], F32)) for i in range(2)]

            self.dma(ph, "sync", "p3btw", bob_rep[:], d["c_bob"])
            wl = self.dma(ph, "sync", "p3btw", wosb[:].rearrange("p (fc o) -> p fc o", fc=8),
                          d["woTb"].rearrange("(fc p) o -> p fc o", p=128))
            ph.wait("tensor", wl)
            ph.wait("vector", wl)

            psY_free = [0, 0]
            xin_free = [(sD, 0), (sD, 0)]
            xo_free = [(sD, 0), (sD, 0)]
            for tt in range(NT):
                bb = tt % 2
                xld = self.dma(ph, "sync", f"p3btx{bb}", xin[bb][:],
                               d["x2s"][tt * 128:(tt + 1) * 128, :],
                               waits=[xin_free[bb]])
                lasty = 0
                for fc in range(8):
                    w = []
                    if fc == 0:
                        w = [(sD, psY_free[0]), (sD, psY_free[1])]
                    ph.op("tensor", lambda e, fc=fc, tt=tt: e.matmul(
                        psY[0][:], rT[:, fc * 4096 + tt * 128:fc * 4096 + (tt + 1) * 128],
                        wosb[:, fc * 1024:fc * 1024 + 512],
                        start=(fc == 0), stop=(fc == 7)), inc=sP, waits=w)
                    lasty = ph.op("tensor", lambda e, fc=fc, tt=tt: e.matmul(
                        psY[1][:], rT[:, fc * 4096 + tt * 128:fc * 4096 + (tt + 1) * 128],
                        wosb[:, fc * 1024 + 512:(fc + 1) * 1024],
                        start=(fc == 0), stop=(fc == 7)), inc=sP)
                ev = 0
                for eb in range(2):
                    ph.op("vector", lambda e, eb=eb, bb=bb: e.scalar_tensor_tensor(
                        xo[bb][:, eb * 512:(eb + 1) * 512], psY[eb][:], 1.0,
                        bob_rep[:, eb * 512:(eb + 1) * 512], OP.mult, OP.add),
                        waits=[(sP, lasty), xld, xo_free[bb]] if eb == 0 else ())
                    ev = ph.op("vector", lambda e, eb=eb, bb=bb: e.tensor_tensor(
                        xo[bb][:, eb * 512:(eb + 1) * 512], xo[bb][:, eb * 512:(eb + 1) * 512],
                        xin[bb][:, eb * 512:(eb + 1) * 512], OP.add), inc=sD)
                psY_free[0] = ev
                psY_free[1] = ev
                xin_free[bb] = (sD, ev)
                xo_free[bb] = self.dma(ph, "gpsimd", f"p3bty{bb}",
                                       d["x3s"][tt * 128:(tt + 1) * 128, :],
                                       xo[bb][:], waits=[(sD, ev)])
            ph.realize()

    def build(self):
        self.es12 = ExitStack()
        self.es3 = ExitStack()
        if "p0" in self.phases:
            self.p0()
        if "p1a" in self.phases:
            self.p1a()
        if "p1b" in self.phases:
            self.p1b()
        self.es12.close()
        if "p2" in self.phases:
            self.ffn("p2", "x1s", "x2s")
        if "p3a" in self.phases:
            self.p3a()
        if "p3b" in self.phases:
            self.p3b()
        self.es3.close()
        if "p4" in self.phases:
            self.ffn("p4", "x3s", "y")
        ph = Phase(self, "fin")
        for nm, sm in self._sems.items():
            if nm.startswith("dma_") and sm.n > 0:
                ph.wait("gpsimd", sm, sm.n)
        ph.realize()
        return self.nc


def _consts(inputs):
    identbf = np.eye(128, dtype=ml_dtypes.bfloat16)
    identf = np.eye(128, dtype=np.float32)
    i = np.arange(128)
    keep = (i[None, :] > i[:, None] + 1)
    mm = keep.astype(np.float32)
    mb = np.where(keep, np.float32(0.0), np.float32(MASKVAL)).astype(np.float32)
    mm1 = np.ones((128, 128), np.float32)
    mm1[127, 0] = 0.0
    mb1 = np.zeros((128, 128), np.float32)
    mb1[127, 0] = MASKVAL
    f32 = lambda k: np.asarray(inputs[k], dtype=np.float32)
    bqk = np.zeros((128, 128), np.float32)
    bqk[:, 0:8] = f32("L0_attn_a_bq").reshape(8, 128).T
    bqk[:, 8:16] = f32("L0_attn_a_bk").reshape(8, 128).T
    convw = np.zeros((128, 128), np.float32)
    convw[:, 0:96] = f32("L0_ffn_conv_w").reshape(32, 128, 3).transpose(1, 0, 2).reshape(128, 96)
    convb = np.zeros((128, 128), np.float32)
    convb[:, 0:32] = f32("L0_ffn_conv_b").reshape(32, 128).T
    boa = np.ascontiguousarray(np.broadcast_to(f32("L0_attn_a_bo")[None, :], (128, DM)))
    bob = np.ascontiguousarray(np.broadcast_to(f32("L0_attn_b_bo")[None, :], (128, DM)))
    bva = np.ascontiguousarray(np.broadcast_to(f32("L0_attn_a_bv")[None, :], (128, DM)))
    return dict(c_identbf=identbf, c_identf=identf, c_mm=mm, c_mb=mb,
                c_mm1=mm1, c_mb1=mb1, c_bqk=bqk,
                c_convw=convw, c_convb=convb, c_boa=boa, c_bob=bob, c_bva=bva)


_IN_MAP = {
    "a_wq": "L0_attn_a_wq", "a_bq": "L0_attn_a_bq",
    "a_wk": "L0_attn_a_wk", "a_bk": "L0_attn_a_bk",
    "a_wv": "L0_attn_a_wv", "a_bv": "L0_attn_a_bv",
    "a_wo": "L0_attn_a_wo", "a_bo": "L0_attn_a_bo",
    "b_Wq": "L0_attn_b_Wq", "b_Wk": "L0_attn_b_Wk",
    "b_Wv": "L0_attn_b_Wv", "b_Wo": "L0_attn_b_Wo",
    "b_bo": "L0_attn_b_bo",
    "f_w1": "L0_ffn_w1", "f_cw": "L0_ffn_conv_w",
    "f_cb": "L0_ffn_conv_b", "f_w2": "L0_ffn_w2",
}


def kernel(**inputs):
    b = Build()
    nc = b.build()
    base = {k: np.ascontiguousarray(np.asarray(inputs[v], dtype=np.float32))
            for k, v in _IN_MAP.items()}
    base.update(_consts(inputs))
    x = np.asarray(inputs["L0_attn_a_x"], dtype=np.float32)
    in_maps = []
    for core in range(8):
        m = dict(base)
        m["x"] = np.ascontiguousarray(x[core])
        in_maps.append(m)
    res = run_bass_kernel_spmd(nc, in_maps, core_ids=list(range(8)))
    return np.stack([res.results[i]["y"] for i in range(8)], axis=0)
